# version 13 (final)
# baseline (speedup 1.0000x reference)
import numpy as np
import ml_dtypes
import concourse.bass as bass
import concourse.mybir as mybir
from concourse.bass_utils import run_bass_kernel_spmd

F32 = mybir.dt.float32
BF16 = mybir.dt.bfloat16
AF = mybir.ActivationFunctionType
ALU = mybir.AluOpType
AX = mybir.AxisListType

S_TOK = 2048
D = 2048
NH = 16
NDC = 16
NTT = 16
NQG = 4
NPASS = 4
SCALE = 1.0 / (128.0 ** 0.25)
EPS = 1e-5
NRM = float(1.0 / np.sqrt(8192.0))
NEGBIG = -1.0e30
POOL_B = 73728

ENGS = ("sync", "gpsimd", "tensor", "vector", "scalar")
TAPS = {}
PHASE_LIMIT = [None]


class Sch:
    def __init__(self):
        self.ops = {e: [] for e in ENGS}
        self.cnt = {}

    def emit(self, eng, fn, waits=(), inc=None, n=1):
        ws = tuple((s, v) for (s, v) in waits if v is not None and v > 0)
        self.ops[eng].append((fn, ws, inc, n))
        if inc is None:
            return None
        self.cnt[inc] = self.cnt.get(inc, 0) + n
        return self.cnt[inc]

    def c(self, sem):
        return self.cnt.get(sem, 0)


class Guard:
    def __init__(self):
        self.m = {}

    def w(self, *keys):
        return [self.m[k] for k in keys if k in self.m]

    def set(self, key, sem, val):
        self.m[key] = (sem, val)


def _hadamard(n):
    h = np.array([[1.0]], dtype=np.float32)
    while h.shape[0] < n:
        h = np.block([[h, h], [h, -h]])
    return h.astype(np.float32)


def build_program():
    nc = bass.Bass("TRN2", target_bir_lowering=False, debug=False,
                   detect_race_conditions=False)

    x_t = nc.dram_tensor("x", [S_TOK, D], F32, kind="ExternalInput")
    wqb_t = nc.dram_tensor("wqb", [NH, 128, NDC, 128], F32, kind="ExternalInput")
    wkb_t = nc.dram_tensor("wkb", [NH, 128, NDC, 128], F32, kind="ExternalInput")
    wvb_t = nc.dram_tensor("wvb", [4, 128, NDC, 512], F32, kind="ExternalInput")
    wob_t = nc.dram_tensor("wob", [128, NDC, D], F32, kind="ExternalInput")
    w1b_t = nc.dram_tensor("w1b", [64, 128, NDC, 128], F32, kind="ExternalInput")
    w3b_t = nc.dram_tensor("w3b", [64, 128, NDC, 128], F32, kind="ExternalInput")
    w2p_t = nc.dram_tensor("w2p", [64, 128, D], F32, kind="ExternalInput")
    bq2_t = nc.dram_tensor("bq2", [128, NH], F32, kind="ExternalInput")
    bk2_t = nc.dram_tensor("bk2", [128, NH], F32, kind="ExternalInput")
    bv2_t = nc.dram_tensor("bv2", [128, NH], F32, kind="ExternalInput")
    bo_t = nc.dram_tensor("bo_r", [1, D], F32, kind="ExternalInput")
    idm_t = nc.dram_tensor("idm", [128, 128], F32, kind="ExternalInput")
    mask_t = nc.dram_tensor("maskf", [128, 128], F32, kind="ExternalInput")
    h128_t = nc.dram_tensor("h128f", [128, 128], F32, kind="ExternalInput")
    bd64_t = nc.dram_tensor("bd64f", [128, 128], F32, kind="ExternalInput")
    y_t = nc.dram_tensor("y", [S_TOK, D], F32, kind="ExternalOutput")

    def scratch(name, shape, dtype):
        kind = "ExternalOutput" if TAPS.get(name) else "Internal"
        return nc.dram_tensor(name, shape, dtype, kind=kind)

    x1_t = scratch("x1", [S_TOK, D], F32)
    xn1_t = scratch("xn1", [D, S_TOK], BF16)
    w2d_t = scratch("w2d", [8192, D], BF16)

    tap_outs = {}
    for nm, shp in (("xn", [128, NDC, S_TOK]), ("o", [128, NDC, S_TOK]),
                    ("qh", [128, S_TOK]), ("kh", [128, S_TOK]),
                    ("vg", [128, NTT, 512]), ("hid2", [128, NDC, S_TOK])):
        if TAPS.get(nm):
            tap_outs[nm] = nc.dram_tensor("tap_" + nm, shp, BF16, kind="ExternalOutput")

    big0 = nc.alloc_sbuf_tensor("big0", [128, NDC, S_TOK], BF16)
    big1 = nc.alloc_sbuf_tensor("big1", [128, NDC, S_TOK], BF16)
    pool = nc.alloc_sbuf_tensor("pool", [128, POOL_B // 2], BF16)

    idb = nc.alloc_sbuf_tensor("idb", [128, 128], BF16)
    maskb = nc.alloc_sbuf_tensor("maskb", [128, 128], BF16)
    h128b = nc.alloc_sbuf_tensor("h128b", [128, 128], BF16)
    bd64b = nc.alloc_sbuf_tensor("bd64b", [128, 128], BF16)
    onesb = nc.alloc_sbuf_tensor("onesb", [128, 128], BF16)
    bqs = nc.alloc_sbuf_tensor("bqs", [128, NH], F32)
    bks = nc.alloc_sbuf_tensor("bks", [128, NH], F32)
    bvb = nc.alloc_sbuf_tensor("bvb", [128, NH], BF16)
    stats = nc.alloc_sbuf_tensor("stats", [128, 16], F32)
    idf = nc.alloc_sbuf_tensor("idf", [128, 128], F32)

    def pv(off, shape, dtype):
        nel = 1
        for s in shape[1:]:
            nel *= s
        if dtype == BF16:
            ap = pool[0:shape[0], off // 2:off // 2 + nel]
        else:
            ap = pool[0:shape[0], off // 2:off // 2 + nel * 2].bitcast(dtype)
        if len(shape) == 3:
            ap = ap.rearrange("p (a b) -> p a b", a=shape[1])
        elif len(shape) == 4:
            ap = ap.rearrange("p (a b c) -> p a b c", a=shape[1], b=shape[2])
        return ap

    xt = pv(0, [128, 2, D], F32)
    sqt = pv(16384, [128, D], F32)
    vg = pv(0, [128, NTT, 512], BF16)
    qh = pv(16384, [128, S_TOK], BF16)
    kh = pv(20480, [128, S_TOK], BF16)
    pp = pv(24576, [128, 2, 512], BF16)
    mtmp = pv(26624, [128, 64], F32)
    mb1 = pv(69632, [1, 16], BF16)
    zrow = pv(72704, [1, 512], BF16)
    recs = pv(27648, [128, 512], F32)
    t1s = pv(29696, [128, 512], F32)
    w2in = pv(31744, [128, D], BF16)
    w2st = pv(35840, [128, D], BF16)
    wvg = pv(39936, [128, NDC, 512], BF16)
    wqh = pv(56320, [128, NDC, 128], BF16)
    wkh = pv(60416, [128, NDC, 128], BF16)
    pxs = pv(64512, [128, 2, 512], F32)
    xd = pv(24576, [128, D], F32)
    x1t = pv(32768, [128, D], F32)
    xn1st = pv(45056, [128, NDC, 512], BF16)
    bo_row = pv(61440, [1, D], F32)
    bo_eff = pv(69632, [1, D], BF16)
    w1c = pv(0, [128, 2, NDC, 128], BF16)
    w3c = pv(8192, [128, 2, NDC, 128], BF16)
    silu_sb = pv(16384, [128, 2, 512], F32)
    hidt = pv(20480, [128, 2, 512], BF16)
    ytile = pv(22528, [128, 2, D], F32)
    outt = pv(38912, [128, 2, D], F32)

    pb = [nc.alloc_psum_tensor("pb%d" % i, [128, 512], F32) for i in range(8)]

    sch = Sch()
    gd = Guard()
    E = sch.emit
    x = x_t.ap()
    y = y_t.ap()
    x1 = x1_t.ap()

    def dma(eng, sem, dst_ap, src_ap, waits=()):
        tok = E(eng, lambda e, d=dst_ap, s=src_ap: e.dma_start(d, s),
                waits=waits, inc=sem, n=16)
        return (sem, tok)

    dma("gpsimd", "CG", idb[:, :], idm_t.ap())
    dma("gpsimd", "CG", maskb[:, :], mask_t.ap())
    dma("gpsimd", "CG", h128b[:, :], h128_t.ap())
    dma("gpsimd", "CG", bd64b[:, :], bd64_t.ap())
    dma("gpsimd", "CG", bvb[:, :], bv2_t.ap())
    CGF = ("CG", 80)
    E("gpsimd", lambda g: g.memset(zrow[0:1, :], 0.0), inc="G")
    tok_consts = E("gpsimd", lambda g: g.memset(onesb[:, :], 1.0), inc="G")
    dma("sync", "CS", idf[:, :], idm_t.ap())
    dma("sync", "CS", bqs[:, :], bq2_t.ap())
    dma("sync", "CS", bks[:, :], bk2_t.ap())
    CSF = ("CS", 48)
    E("vector", lambda v: v.tensor_scalar(bqs[:, :], bqs[:, :], SCALE, None, ALU.mult),
      waits=[CSF], inc="V")
    E("vector", lambda v: v.tensor_scalar(bks[:, :], bks[:, :], SCALE, None, ALU.mult), inc="V")

    for tt in range(NTT):
        b = tt % 2
        xs = "X%d" % b
        xtb = xt[:, b, :]
        wx = dma("sync", xs, xtb, x[tt * 128:(tt + 1) * 128, :], waits=gd.w(("xt", b)))
        tok_r1 = E("vector", lambda v, xtb=xtb: v.tensor_reduce(stats[:, 0:1], xtb, AX.X, ALU.add),
                   waits=[wx], inc="V")
        tok_m = E("vector", lambda v: v.tensor_scalar(stats[:, 1:2], stats[:, 0:1], 1.0 / D, None, ALU.mult),
                  waits=[("V", tok_r1)], inc="V")
        E("vector", lambda v, xtb=xtb: v.tensor_scalar(xtb, xtb, stats[:, 1:2], None, ALU.subtract),
          waits=[("V", tok_m)])
        E("vector", lambda v, xtb=xtb: v.tensor_mul(sqt[:, :], xtb, xtb),
          waits=gd.w("sqt_xn"))
        tok_r2 = E("vector", lambda v: v.tensor_reduce(stats[:, 2:3], sqt[:, :], AX.X, ALU.add), inc="V")
        tok_var = E("vector", lambda v: v.tensor_scalar(stats[:, 3:4], stats[:, 2:3], 1.0 / D, EPS,
                                                        ALU.mult, ALU.add),
                    waits=[("V", tok_r2)], inc="V")
        tok_std = E("scalar", lambda s: s.activation(stats[:, 4:5], stats[:, 3:4], AF.Sqrt),
                    waits=[("V", tok_var)], inc="S")
        tok_rc = E("vector", lambda v: v.reciprocal(stats[:, 5:6], stats[:, 4:5]),
                   waits=[("S", tok_std)], inc="V")
        tok_xnt = E("vector", lambda v, xtb=xtb: v.tensor_scalar(sqt[:, :], xtb, stats[:, 5:6], None, ALU.mult),
                    waits=[("V", tok_rc)], inc="V")
        gd.set(("xt", b), "V", tok_xnt)
        last_tr = None
        for i in range(NDC):
            kb = 6 + (i % 2)
            tok_tr = E("tensor", lambda t, i=i, kb=kb: t.transpose(
                pb[kb][:, 0:128], sqt[:, i * 128:(i + 1) * 128], idf[:, :]),
                waits=[("V", tok_xnt), CSF] + gd.w(("pb", kb)), inc="T")
            tok_ev = E("scalar", lambda s, i=i, tt=tt, kb=kb: s.copy(
                big0[:, i, tt * 128:(tt + 1) * 128], pb[kb][:, 0:128]),
                waits=[("T", tok_tr)], inc="S")
            gd.set(("pb", kb), "S", tok_ev)
            last_tr = tok_tr
        gd.set("sqt_xn", "T", last_tr)
    tok_xn_ready_S = sch.c("S")

    tap_waits = {}
    if "xn" in tap_outs:
        tap_waits["xn"] = dma("sync", "TP", tap_outs["xn"].ap(), big0[:, :, :],
                              waits=[("S", tok_xn_ready_S)])

    if PHASE_LIMIT[0] == "ln1":
        finals = [(sem, sch.c(sem)) for sem in ("TP",) if sch.c(sem) > 0]
        return _finish(nc, sch, finals)

    prep_state = {"u": 0}

    def emit_prep_chunk():
        u = prep_state["u"]
        if u >= 64:
            return
        prep_state["u"] = u + 1
        win = dma("gpsimd", "PW", w2in[:, :], w2p_t.ap()[u], waits=gd.w("w2in"))
        ev_last = None
        for dg in range(4):
            kb = 6 + (dg % 2)
            tok_mm = E("tensor", lambda t, dg=dg, kb=kb: t.matmul(
                pb[kb][:, :], h128b[:, :], w2in[:, dg * 512:(dg + 1) * 512],
                start=True, stop=True),
                waits=[win, CGF] + gd.w(("pb", kb)) + (gd.w("w2st") if dg == 0 else []),
                inc="T")
            ev_last = E("vector", lambda v, dg=dg, kb=kb: v.tensor_scalar(
                w2st[:, dg * 512:(dg + 1) * 512], pb[kb][:, :], NRM, None, ALU.mult),
                waits=[("T", tok_mm)], inc="V")
            gd.set(("pb", kb), "V", ev_last)
        gd.set("w2in", "T", sch.c("T"))
        out_ap = bass.AP(w2d_t, u * D, [[64 * D, 128], [1, D]])
        wst = dma("sync", "PS", out_ap, w2st[:, :], waits=[("V", ev_last)])
        gd.set("w2st", "PS", wst[1])

    tok_xn_last_T = None
    wv_pref = {}
    for g in range(4):
        if g not in wv_pref:
            wv_pref[g] = dma("gpsimd", "WV", wvg[:, :, :], wvb_t.ap()[g], waits=gd.w("wvg"))
        wwv = wv_pref.pop(g)
        for tt in range(NTT):
            kb = 6 + (tt % 2)
            tok_mm = None
            for dc in range(NDC):
                tok_mm = E("tensor", lambda t, tt=tt, dc=dc, kb=kb: t.matmul(
                    pb[kb][:, :], big0[:, dc, tt * 128:(tt + 1) * 128], wvg[:, dc, :],
                    start=(dc == 0), stop=(dc == NDC - 1)),
                    waits=([("S", tok_xn_ready_S), wwv] + gd.w(("pb", kb)))
                          if dc == 0 else [],
                    inc="T" if dc == NDC - 1 else None)
            tok_ev = E("vector", lambda v, tt=tt, kb=kb: v.tensor_copy(vg[:, tt, :], pb[kb][:, :]),
                       waits=[("T", tok_mm)] + (gd.w("vg") if tt == 0 else [])
                             + ([tap_waits["vg"]] if (g == 1 and tt == 0 and "vg" in tap_waits) else []),
                       inc="V")
            gd.set(("pb", kb), "V", tok_ev)
        tok_vg_ready = sch.c("V")
        gd.set("wvg", "T", sch.c("T"))
        if g + 1 < 4:
            wv_pref[g + 1] = dma("gpsimd", "WV", wvg[:, :, :], wvb_t.ap()[g + 1], waits=gd.w("wvg"))
        if g == 0 and "vg" in tap_outs:
            tap_waits["vg"] = dma("sync", "TP", tap_outs["vg"].ap(), vg[:, :, :],
                                  waits=[("V", tok_vg_ready)])

        for h in range(g * 4, g * 4 + 4):
            if h == 0:
                prefetch = {}
            if h not in prefetch:
                prefetch[h] = (dma("gpsimd", "QW", wqh[:, :, :], wqb_t.ap()[h], waits=gd.w("wqh")),
                               dma("gpsimd", "KW", wkh[:, :, :], wkb_t.ap()[h], waits=gd.w("wkh")))
            wq, wk = prefetch.pop(h)

            def proj_qg(wsb, ww, dst, bias_sb, qg, h=h, base=6):
                kb = base + (qg % 2)
                tok_mm = None
                for dc in range(NDC):
                    tok_mm = E("tensor", lambda t, wsb=wsb, dc=dc, qg=qg, kb=kb: t.matmul(
                        pb[kb][:, :], wsb[:, dc, :], big0[:, dc, qg * 512:(qg + 1) * 512],
                        start=(dc == 0), stop=(dc == NDC - 1)),
                        waits=([ww] + gd.w(("pb", kb))) if dc == 0 else [],
                        inc="T" if dc == NDC - 1 else None)
                tok_ev = E("scalar", lambda s, dst=dst, qg=qg, kb=kb, bias_sb=bias_sb, h=h: s.activation(
                    dst[:, qg * 512:(qg + 1) * 512], pb[kb][:, :], AF.Identity,
                    bias=bias_sb[:, h:h + 1], scale=SCALE),
                    waits=[("T", tok_mm)]
                          + ([tap_waits["qh"]] if (h == 1 and qg == 0 and "qh" in tap_waits) else []),
                    inc="S")
                gd.set(("pb", kb), "S", tok_ev)
                return tok_ev

            E("vector", lambda v: v.memset(mtmp[:, :], NEGBIG))
            for qg in range(NQG):
                proj_qg(wqh, wq, qh, bqs, qg, base=0)
            gd.set("wqh", "T", sch.c("T"))
            qev_tok = sch.c("S")
            tok_flush = None
            mp_flush = {}
            for qg in range(NQG):
                kev = proj_qg(wkh, wk, kh, bks, qg)
                for ti in range(qg * 4, qg * 4 + 4):
                    kext = (ti + 1) * 128
                    for kcc in range((kext + 511) // 512):
                        w = min(512, kext - kcc * 512)
                        kb = 6 + ((ti + kcc + 1) % 2)
                        tok_mm = E("tensor", lambda t, ti=ti, kcc=kcc, w=w, kb=kb: t.matmul(
                            pb[kb][:, 0:w], qh[:, ti * 128:(ti + 1) * 128],
                            kh[:, kcc * 512:kcc * 512 + w], start=True, stop=True),
                            waits=[("S", max(qev_tok, kev))] + gd.w(("pb", kb)), inc="T")
                        tok_flush = E("vector", lambda v, ti=ti, kcc=kcc, w=w, kb=kb: v.tensor_reduce(
                            mtmp[:, ti * 4 + kcc:ti * 4 + kcc + 1], pb[kb][:, 0:w], AX.X, ALU.max),
                            waits=[("T", tok_mm)], inc="V")
                        gd.set(("pb", kb), "V", tok_flush)
                mp_flush[qg] = tok_flush
            gd.set("wkh", "T", sch.c("T"))
            tok_qk_ready = sch.c("S")
            tok_xn_last_T = sch.c("T")
            if h + 1 < NH:
                prefetch[h + 1] = (dma("gpsimd", "QW", wqh[:, :, :], wqb_t.ap()[h + 1], waits=gd.w("wqh")),
                                   dma("gpsimd", "KW", wkh[:, :, :], wkb_t.ap()[h + 1], waits=gd.w("wkh")))

            if h == 0 and "qh" in tap_outs:
                tap_waits["qh"] = dma("sync", "TP", tap_outs["qh"].ap(), qh[:, :],
                                      waits=[("S", tok_qk_ready)])
                tap_waits["qh"] = ("TP", dma("sync", "TP", tap_outs["kh"].ap(), kh[:, :],
                                             waits=[("S", tok_qk_ready)])[1])

            tok_t1 = None
            for qg in range(NQG):
                tok_t1 = E("vector", lambda v, qg=qg: v.tensor_reduce(
                    stats[:, 12 + qg:13 + qg], mtmp[:, qg * 16:(qg + 1) * 16], AX.X, ALU.max),
                    waits=([("V", mp_flush[3])] + gd.w("bchain")) if qg == 0 else [],
                    inc="V" if qg == NQG - 1 else None)
            tok_t2 = E("tensor", lambda t: t.transpose(
                pb[6][0:4, 0:128], stats[:, 12:16], idf[:, :]),
                waits=[("V", tok_t1)] + gd.w(("pb", 6)), inc="T")
            gd.set("bchain", "T", tok_t2)
            tok_t3 = E("vector", lambda v: v.tensor_reduce(
                stats[0:4, 3:4], pb[6][0:4, 0:128], AX.X, ALU.max),
                waits=[("T", tok_t2)] + gd.w("st34"), inc="V")
            gd.set(("pb", 6), "V", tok_t3)
            tok_t4 = E("tensor", lambda t: t.transpose(
                pb[7][0:1, 0:4], stats[0:4, 3:4], idf[0:4, 0:4]),
                waits=[("V", tok_t3)] + gd.w(("pb", 7)), inc="T")
            gd.set("st34", "T", tok_t4)
            tok_t5 = E("vector", lambda v: v.tensor_scalar(
                mb1[0:1, 0:4], pb[7][0:1, 0:4], -1.0, None, ALU.mult),
                waits=[("T", tok_t4)] + gd.w("mb1"), inc="V")
            gd.set(("pb", 7), "V", tok_t5)
            tok_t6 = E("tensor", lambda t: t.matmul(
                pb[6][:, 0:4], onesb[0:1, :], mb1[0:1, 0:4], start=True, stop=True),
                waits=[("V", tok_t5), ("G", tok_consts)] + gd.w(("pb", 6)), inc="T")
            gd.set("mb1", "T", tok_t6)
            wn = []
            for qg in range(NQG):
                wn += gd.w(("negc", qg))
            tok_t7 = E("vector", lambda v: v.tensor_copy(stats[:, 8:12], pb[6][:, 0:4]),
                       waits=[("T", tok_t6)] + wn, inc="V")
            gd.set(("pb", 6), "V", tok_t7)
            for qg in range(NQG):
                gd.set(("negcv", qg), "V", tok_t7)

            carry = getattr(emit_prep_chunk, "_carry", None)
            if carry is None or carry.get("h") != h:
                carry = {"h": h, 0: {"cp": None, "av": None}, 1: {"cp": None, "av": None}}
            emit_prep_chunk._carry = carry
            for pair in range(2):
                streams = []
                for sidx in range(2):
                    qg = pair * 2 + sidx
                    st_state = {
                        "qg": qg, "s": sidx, "kc": 0, "emitted": 0, "nkc": 4 * (qg + 1),
                        "avb": 4 + sidx, "smb": 6 + sidx,
                        "av_tok": carry[sidx]["av"], "cp_tok": carry[sidx]["cp"],
                        "st_toks": {}, "exp_toks": {}, "cp_toks": {},
                    }
                    for zb in (st_state["avb"], st_state["smb"]):
                        E("tensor", lambda t, zb=zb: t.matmul(
                            pb[zb][:, :], onesb[0:1, :], zrow[0:1, :], start=True, stop=False),
                            waits=[("G", tok_consts)] + gd.w(("pb", zb)), inc=None)
                    streams.append(st_state)

                def emit_st(z, h=h, g=g):
                    qg, sx, kc = z["qg"], z["s"], z["emitted"]
                    stb = 2 * sx + (kc % 2)
                    off = max(0, kc * 128 - qg * 512)
                    wdt = 512 - off
                    is_diag = (kc // 4 == qg)
                    tok_st = E("tensor", lambda t, kc=kc, qg=qg, stb=stb, off=off, wdt=wdt, is_diag=is_diag: t.matmul(
                        pb[stb][:, off:off + wdt], kh[:, kc * 128:(kc + 1) * 128],
                        qh[:, qg * 512 + off:qg * 512 + off + wdt],
                        start=True, stop=not is_diag),
                        waits=gd.w(("pb", stb)) + ([("V", sch.c("V")), CGF] if kc == 0 and qg == 0 else []),
                        inc=None if is_diag else "T")
                    if is_diag:
                        tok_st = E("tensor", lambda t, stb=stb, off=off: t.matmul(
                            pb[stb][:, off:off + 128], idb[:, :], maskb[:, :],
                            start=False, stop=True), inc="T")
                    z["st_toks"][kc] = tok_st
                    wexp = [("T", tok_st)]
                    if kc == 0:
                        wexp += gd.w(("negcv", qg))
                    if z["cp_tok"]:
                        wexp.append(("V", z["cp_tok"]))
                    tok_exp = E("scalar", lambda s, kc=kc, sx=sx, stb=stb, off=off, wdt=wdt, qg=qg: s.activation(
                        pxs[:, sx, 0:wdt], pb[stb][:, off:off + wdt], AF.Exp,
                        bias=stats[:, 8 + qg:9 + qg], scale=1.0),
                        waits=wexp, inc="S")
                    gd.set(("pb", stb), "S", tok_exp)
                    gd.set(("negc", qg), "S", tok_exp)
                    wcp = [("S", tok_exp)]
                    if z["av_tok"]:
                        wcp.append(("T", z["av_tok"]))
                    tok_cp2 = E("vector", lambda v, sx=sx, wdt=wdt: v.tensor_copy(
                        pp[:, sx, 0:wdt], pxs[:, sx, 0:wdt]),
                        waits=wcp, inc="V")
                    z["cp_tok"] = tok_cp2
                    z["cp_toks"][kc] = tok_cp2
                    z["emitted"] += 1

                def emit_acc(z, h=h, g=g):
                    qg, sx, kc, nkc = z["qg"], z["s"], z["kc"], z["nkc"]
                    avb, smb = z["avb"], z["smb"]
                    off = max(0, kc * 128 - qg * 512)
                    wdt = 512 - off
                    tok_cp2 = z["cp_toks"].pop(kc)
                    E("tensor", lambda t, sx=sx, smb=smb, off=off, wdt=wdt, kc=kc, nkc=nkc: t.matmul(
                        pb[smb][:, off:off + wdt], onesb[:, :], pp[:, sx, 0:wdt],
                        start=False, stop=(kc == nkc - 1)),
                        waits=[("V", tok_cp2)], inc=None)
                    tok_av = E("tensor", lambda t, sx=sx, avb=avb, off=off, wdt=wdt, kc=kc, nkc=nkc, h=h, g=g: t.matmul(
                        pb[avb][:, off:off + wdt], vg[:, kc, (h - g * 4) * 128:(h - g * 4 + 1) * 128],
                        pp[:, sx, 0:wdt],
                        start=False, stop=(kc == nkc - 1)),
                        waits=([("V", tok_vg_ready)] if kc == 0 else []), inc="T")
                    z["av_tok"] = tok_av
                    z["kc"] += 1

                def emit_tail(z, h=h):
                    qg, avb, smb = z["qg"], z["avb"], z["smb"]
                    tok_avstop = z["av_tok"]
                    tok_cp = E("vector", lambda v, smb=smb: v.tensor_scalar(
                        t1s[:, :], pb[smb][:, :], 1e-38, None, ALU.max),
                        waits=[("T", tok_avstop)] + gd.w("t1s"), inc="V")
                    gd.set(("pb", smb), "V", tok_cp)
                    tok_rc = E("vector", lambda v: v.reciprocal(recs[:, :], t1s[:, :]),
                               waits=[("V", tok_cp)] + gd.w("recs"), inc="V")
                    tok_oev = E("vector", lambda v, h=h, qg=qg, avb=avb: v.tensor_tensor(
                        big1[:, h, qg * 512:(qg + 1) * 512], pb[avb][:, :], recs[:, :], ALU.mult),
                        waits=[("V", tok_rc), ("T", tok_avstop)], inc="V")
                    gd.set(("pb", avb), "V", tok_oev)
                    gd.set("t1s", "V", tok_rc)
                    gd.set("recs", "V", tok_oev)

                za, zb_ = streams
                for z in streams:
                    if z["emitted"] < z["nkc"]:
                        emit_st(z)
                while za["kc"] < za["nkc"] or zb_["kc"] < zb_["nkc"]:
                    for z in (za, zb_):
                        if z["emitted"] < z["nkc"]:
                            emit_st(z)
                        if z["kc"] < z["emitted"]:
                            emit_acc(z)
                emit_tail(za)
                emit_tail(zb_)
                for z in streams:
                    carry[z["s"]]["cp"] = z["cp_tok"]
                    carry[z["s"]]["av"] = z["av_tok"]
            for _ in range(4):
                emit_prep_chunk()
    tok_o_done_V = sch.c("V")
    tok_attn_T = sch.c("T")
    PSF = ("PS", sch.c("PS"))

    if "o" in tap_outs:
        tap_waits["o"] = dma("sync", "TP", tap_outs["o"].ap(), big1[:, :, :],
                             waits=[("V", tok_o_done_V)])
    if PHASE_LIMIT[0] == "attn":
        finals = [(sem, sch.c(sem)) for sem in ("TP", "PS") if sch.c(sem) > 0]
        return _finish(nc, sch, finals)

    wo_waits = []
    for c4 in range(4):
        in_ap = bass.AP(wob_t, c4 * 4 * D, [[NDC * D, 128], [D, 4], [1, D]])
        wo_waits.append(dma("gpsimd", "WO%d" % c4, big0[:, c4 * 4:(c4 + 1) * 4, :], in_ap,
                            waits=[("T", tok_xn_last_T)]))

    wbo = dma("sync", "BO", bo_row, bo_t.ap(),
              waits=[("T", tok_attn_T), PSF, ("V", tok_o_done_V)])
    tok_wobv = None
    for og in range(4):
        for ic in range(NDC):
            tok_wobv = E("tensor", lambda t, og=og, ic=ic: t.matmul(
                pb[6][0:1, :], bvb[:, ic:ic + 1], big0[:, ic, og * 512:(og + 1) * 512],
                start=(ic == 0), stop=(ic == NDC - 1)),
                waits=([wo_waits[ic // 4], CGF] + (gd.w(("pb", 6)) if ic == 0 else []))
                      if ic % 4 == 0 else [],
                inc="T" if ic == NDC - 1 else None)
        tok_be = E("vector", lambda v, og=og: v.tensor_tensor(
            bo_eff[0:1, og * 512:(og + 1) * 512], bo_row[0:1, og * 512:(og + 1) * 512],
            pb[6][0:1, :], ALU.add),
            waits=[("T", tok_wobv), wbo], inc="V")
        gd.set(("pb", 6), "V", tok_be)
    tok_boeff = sch.c("V")

    xn1_flush_tok = None
    a4_pend = []
    _a4_state = {"fl": None}

    def _flush_a4tr(item):
        tt, bs, tok_xnt = item
        last_tr = None
        for i in range(NDC):
            kb = bs + (i % 2)
            tok_tr = E("tensor", lambda t, i=i, kb=kb: t.transpose(
                pb[kb][:, 0:128], x1t[:, i * 128:(i + 1) * 128], idf[:, :]),
                waits=[("V", tok_xnt)] + gd.w(("pb", kb)), inc="T")
            tok_ev2 = E("scalar", lambda s, i=i, tt=tt, kb=kb: s.copy(
                xn1st[:, i, (tt % 4) * 128:(tt % 4 + 1) * 128], pb[kb][:, 0:128]),
                waits=[("T", tok_tr)] + (gd.w("xn1st") if i == 0 else []), inc="S")
            gd.set(("pb", kb), "S", tok_ev2)
            last_tr = tok_tr
        gd.set(("x1t", "tr"), "T", last_tr)
        if tt % 4 == 3:
            tb = tt // 4
            out_ap = bass.AP(xn1_t, tb * 512, [[S_TOK, 128], [128 * S_TOK, NDC], [1, 512]])
            wfl = dma("sync", "SN", out_ap, xn1st[:, :, :], waits=[("S", sch.c("S"))])
            gd.set("xn1st", "SN", wfl[1])
            _a4_state["fl"] = wfl[1]

    for tt in range(NTT):
        b = tt % 2
        xs = "X%d" % b
        xtb = xt[:, b, :]
        wx = dma("sync", xs, xtb, x[tt * 128:(tt + 1) * 128, :],
                 waits=gd.w(("xt", b)) + ([("T", tok_attn_T)] if tt < 2 else []))
        bs = 0 if b == 0 else 4
        for og in range(4):
            E("tensor", lambda t, og=og, bs=bs: t.matmul(
                pb[bs + og][:, :], onesb[0:1, :], bo_eff[0:1, og * 512:(og + 1) * 512],
                start=True, stop=False),
                waits=[("V", tok_boeff)] + gd.w(("pb", bs + og)), inc=None)
        tok_mm = None
        for ic in range(NDC):
            for og in range(4):
                tok_mm = E("tensor", lambda t, tt=tt, ic=ic, og=og, bs=bs: t.matmul(
                    pb[bs + og][:, :], big1[:, ic, tt * 128:(tt + 1) * 128],
                    big0[:, ic, og * 512:(og + 1) * 512],
                    start=False, stop=(ic == NDC - 1)),
                    waits=[("V", tok_o_done_V), wo_waits[ic // 4]] if og == 0 and ic % 4 == 0 else [],
                    inc="T" if (ic == NDC - 1 and og == 3) else None)
        if len(a4_pend) > 0:
            _flush_a4tr(a4_pend.pop(0))
        tok_evs = []
        for og in range(4):
            tok_ev = E("vector", lambda v, og=og, bs=bs, xtb=xtb: v.tensor_tensor(
                x1t[:, og * 512:(og + 1) * 512], pb[bs + og][:, :], xtb[:, og * 512:(og + 1) * 512],
                ALU.add),
                waits=[("T", tok_mm), wx] + ((gd.w(("x1t", "st"), ("x1t", "tr")) + [PSF]) if og == 0 else []),
                inc="V")
            gd.set(("pb", bs + og), "V", tok_ev)
            tok_evs.append(tok_ev)
        gd.set(("xt", b), "V", tok_evs[-1])
        wx1 = dma("sync", "SX", x1[tt * 128:(tt + 1) * 128, :], x1t[:, :],
                  waits=[("V", tok_evs[-1])])
        gd.set(("x1t", "st"), "SX", wx1[1])
        tok_r1 = E("vector", lambda v: v.tensor_reduce(stats[:, 0:1], x1t[:, :], AX.X, ALU.add), inc="V")
        tok_m = E("vector", lambda v: v.tensor_scalar(stats[:, 1:2], stats[:, 0:1], 1.0 / D, None, ALU.mult),
                  waits=[("V", tok_r1)], inc="V")
        E("vector", lambda v: v.tensor_scalar(xd[:, :], x1t[:, :], stats[:, 1:2], None, ALU.subtract),
          waits=[("V", tok_m)])
        E("vector", lambda v: v.tensor_mul(sqt[:, :], xd[:, :], xd[:, :]),
          waits=gd.w("sqt_xn"))
        tok_r2 = E("vector", lambda v: v.tensor_reduce(stats[:, 2:3], sqt[:, :], AX.X, ALU.add), inc="V")
        tok_var = E("vector", lambda v: v.tensor_scalar(stats[:, 3:4], stats[:, 2:3], 1.0 / D, EPS,
                                                        ALU.mult, ALU.add),
                    waits=[("V", tok_r2)], inc="V")
        tok_std = E("scalar", lambda s: s.activation(stats[:, 4:5], stats[:, 3:4], AF.Sqrt),
                    waits=[("V", tok_var)], inc="S")
        tok_rc = E("vector", lambda v: v.reciprocal(stats[:, 5:6], stats[:, 4:5]),
                   waits=[("S", tok_std)], inc="V")
        tok_xnt = E("vector", lambda v: v.tensor_scalar(x1t[:, :], xd[:, :], stats[:, 5:6], None, ALU.mult),
                    waits=[("V", tok_rc), wx1], inc="V")
        a4_pend.append((tt, bs, tok_xnt))
    while a4_pend:
        _flush_a4tr(a4_pend.pop(0))

    if PHASE_LIMIT[0] == "a4":
        finals = [(sem, sch.c(sem)) for sem in ("TP", "PS", "SX", "SN") if sch.c(sem) > 0]
        return _finish(nc, sch, finals)

    SXF = ("SX", sch.c("SX"))
    SNF = ("SN", sch.c("SN"))
    tok_big1_free_T = sch.c("T")
    tok_big0_free_T = sch.c("T")
    ffn_fence = [("T", sch.c("T")), ("V", sch.c("V")), ("S", sch.c("S")), SXF, SNF, PSF]

    def _flush_bd(item):
        hc, tg, sb2, bb, tok_hid, first = item
        tok_bd = E("tensor", lambda t, sb2=sb2, bb=bb: t.matmul(
            pb[bb][:, :], bd64b[:, :], hidt[:, sb2, :], start=True, stop=True),
            waits=[("V", tok_hid)] + gd.w(("pb", bb)), inc="T")
        gd.set(("hidt", sb2), "T", tok_bd)
        tok_h2 = E("vector", lambda v, hc=hc, tg=tg, bb=bb: v.tensor_copy(
            big0[:, hc, tg * 512:(tg + 1) * 512], pb[bb][:, :]),
            waits=[("T", tok_bd)] +
                  ([("T", tok_big0_free_T)] if first else []),
            inc="V")
        gd.set(("pb", bb), "V", tok_h2)

    for p in range(NPASS):
        bd_pend = []
        in_ap = bass.AP(xn1_t, 0, [[S_TOK, 128], [128 * S_TOK, NDC], [1, S_TOK]])
        wxn1 = dma("sync", "XN", big1[:, :, :], in_ap,
                   waits=[SNF, ("T", tok_big1_free_T)]
                         + ([tap_waits["o"]] if (p == 0 and "o" in tap_waits) else []))
        assert _a4_state["fl"] is not None
        for hc in range(NDC):
            ch = p * NDC + hc
            hb = hc % 2
            fence = ffn_fence if (p == 0 and hc < 2) else []
            ww1 = dma("gpsimd", "A%d" % hb, w1c[:, hb, :, :], w1b_t.ap()[ch],
                      waits=list(fence) + gd.w(("w1c", hb)))
            ww3 = dma("gpsimd", "B%d" % hb, w3c[:, hb, :, :], w3b_t.ap()[ch],
                      waits=list(fence) + gd.w(("w3c", hb)))
            for tg in range(NQG):
                rot = (hc * 4 + tg) % 2
                gb, lb, bb = rot, 2 + rot, 4 + rot
                tok_g = tok_l = None
                for dc in range(NDC):
                    tok_g = E("tensor", lambda t, hb=hb, dc=dc, tg=tg, gb=gb: t.matmul(
                        pb[gb][:, :], w1c[:, hb, dc, :], big1[:, dc, tg * 512:(tg + 1) * 512],
                        start=(dc == 0), stop=(dc == NDC - 1)),
                        waits=([ww1, wxn1] + gd.w(("pb", gb))) if dc == 0 else [],
                        inc="T" if dc == NDC - 1 else None)
                for dc in range(NDC):
                    tok_l = E("tensor", lambda t, hb=hb, dc=dc, tg=tg, lb=lb: t.matmul(
                        pb[lb][:, :], w3c[:, hb, dc, :], big1[:, dc, tg * 512:(tg + 1) * 512],
                        start=(dc == 0), stop=(dc == NDC - 1)),
                        waits=([ww3] + gd.w(("pb", lb))) if dc == 0 else [],
                        inc="T" if dc == NDC - 1 else None)
                sb2 = rot
                tok_silu = E("scalar", lambda s, sb2=sb2, gb=gb: s.activation(
                    silu_sb[:, sb2, :], pb[gb][:, :], AF.Silu),
                    waits=[("T", tok_g)] + gd.w(("silu", sb2)), inc="S")
                gd.set(("pb", gb), "S", tok_silu)
                tok_hid = E("vector", lambda v, sb2=sb2, lb=lb: v.tensor_tensor(
                    hidt[:, sb2, :], silu_sb[:, sb2, :], pb[lb][:, :], ALU.mult),
                    waits=[("S", tok_silu), ("T", tok_l)] + gd.w(("hidt", sb2)), inc="V")
                gd.set(("pb", lb), "V", tok_hid)
                gd.set(("silu", sb2), "V", tok_hid)
                bd_pend.append((hc, tg, sb2, bb, tok_hid,
                                (p == 0 and hc == 0 and tg == 0)))
                if len(bd_pend) > 1:
                    _flush_bd(bd_pend.pop(0))
            gd.set(("w1c", hb), "T", sch.c("T"))
            gd.set(("w3c", hb), "T", sch.c("T"))
        while bd_pend:
            _flush_bd(bd_pend.pop(0))
        tok_stage1_T = sch.c("T")
        tok_stage1_V = sch.c("V")
        if p == 0 and "hid2" in tap_outs:
            dma("sync", "TP", tap_outs["hid2"].ap(), big0[:, :, :], waits=[("V", tok_stage1_V)])

        w2dl = []
        for q4 in range(4):
            in2 = bass.AP(w2d_t, (p * NDC + q4 * 4) * 128 * D,
                          [[D, 128], [128 * D, 4], [1, D]])
            w2dl.append(dma("sync", "D%d" % q4, big1[:, q4 * 4:(q4 + 1) * 4, :], in2,
                            waits=[("T", tok_stage1_T), PSF]))
        sy_full = ("SY", sch.c("SY"))
        for tc in range(NTT):
            b = tc % 2
            bs = 0 if b == 0 else 4
            ytb = ytile[:, b, :]
            if p == 0:
                wy = dma("sync", "Y%d" % b, ytb, x1[tc * 128:(tc + 1) * 128, :],
                         waits=[SXF] + (ffn_fence if tc < 2 else []) + gd.w(("ytile", b)))
            else:
                wy = dma("sync", "Y%d" % b, ytb, y[tc * 128:(tc + 1) * 128, :],
                         waits=[sy_full] + gd.w(("ytile", b)))
            tok_mm = None
            for hc in range(NDC):
                for dg in range(4):
                    tok_mm = E("tensor", lambda t, tc=tc, hc=hc, dg=dg, bs=bs: t.matmul(
                        pb[bs + dg][:, :], big0[:, hc, tc * 128:(tc + 1) * 128],
                        big1[:, hc, dg * 512:(dg + 1) * 512],
                        start=(hc == 0), stop=(hc == NDC - 1)),
                        waits=([w2dl[hc // 4], ("V", tok_stage1_V)] + gd.w(("pb", bs + dg)))
                              if dg == 0 else [],
                        inc="T" if (hc == NDC - 1 and dg == 3) else None)
            tok_evs = []
            for dg in range(4):
                tok_ev = E("vector", lambda v, dg=dg, bs=bs, ytb=ytb, b=b: v.tensor_tensor(
                    outt[:, b, dg * 512:(dg + 1) * 512], pb[bs + dg][:, :],
                    ytb[:, dg * 512:(dg + 1) * 512], ALU.add),
                    waits=[("T", tok_mm), wy] + (gd.w(("outt", b)) if dg == 0 else []),
                    inc="V")
                gd.set(("pb", bs + dg), "V", tok_ev)
                tok_evs.append(tok_ev)
            gd.set(("ytile", b), "V", tok_evs[-1])
            wsy = dma("sync", "SY", y[tc * 128:(tc + 1) * 128, :], outt[:, b, :],
                      waits=[("V", tok_evs[-1])])
            gd.set(("outt", b), "SY", wsy[1])
        tok_big1_free_T = sch.c("T")
        tok_big0_free_T = sch.c("T")

    finals = [(s, sch.c(s)) for s in ("SY", "SX", "SN", "PS", "TP") if sch.c(s) > 0]
    return _finish(nc, sch, finals)


def _finish(nc, sch, finals):
    names = set(sch.cnt)
    for ops in sch.ops.values():
        for _, ws, _, _ in ops:
            for s, v in ws:
                names.add(s)
    import contextlib
    with contextlib.ExitStack() as stack:
        sems = {nm: stack.enter_context(nc.semaphore(nm)) for nm in sorted(names)}
        block = stack.enter_context(nc.Block())

        def replay(ename, final_waits=()):
            ops = sch.ops[ename]

            def run(eng):
                seen = {}
                for fn, waits, inc, n in ops:
                    for s, v in waits:
                        if seen.get(s, 0) < v:
                            eng.wait_ge(sems[s], v)
                            seen[s] = v
                    ins = fn(eng)
                    if inc is not None:
                        ins.then_inc(sems[inc], n)
                for s, v in final_waits:
                    if seen.get(s, 0) < v:
                        eng.wait_ge(sems[s], v)
            return run

        block.sync(replay("sync", final_waits=finals))
        block.gpsimd(replay("gpsimd"))
        block.tensor(replay("tensor"))
        block.vector(replay("vector"))
        block.scalar(replay("scalar"))

    return nc


def prep_inputs(inputs):
    X = np.ascontiguousarray(np.asarray(inputs["L0_attn_X"], dtype=np.float32))
    Wq = np.asarray(inputs["L0_attn_Wq"], dtype=np.float32)
    Wk = np.asarray(inputs["L0_attn_Wk"], dtype=np.float32)
    Wv = np.asarray(inputs["L0_attn_Wv"], dtype=np.float32)
    Wo = np.asarray(inputs["L0_attn_Wo"], dtype=np.float32)
    W1 = np.asarray(inputs["L0_ffn_W1"], dtype=np.float32)
    W2 = np.asarray(inputs["L0_ffn_W2"], dtype=np.float32)
    W3 = np.asarray(inputs["L0_ffn_W3"], dtype=np.float32)
    bq = np.asarray(inputs["L0_attn_bq"], dtype=np.float32)
    bk = np.asarray(inputs["L0_attn_bk"], dtype=np.float32)
    bv = np.asarray(inputs["L0_attn_bv"], dtype=np.float32)
    bo = np.asarray(inputs["L0_attn_bo"], dtype=np.float32)

    shared = {
        "wqb": np.ascontiguousarray(Wq.reshape(NH, 128, NDC, 128).transpose(0, 3, 2, 1)),
        "wkb": np.ascontiguousarray(Wk.reshape(NH, 128, NDC, 128).transpose(0, 3, 2, 1)),
        "wvb": np.ascontiguousarray(Wv.reshape(4, 512, NDC, 128).transpose(0, 3, 2, 1)),
        "wob": np.ascontiguousarray(Wo.reshape(D, NDC, 128).transpose(2, 1, 0)),
        "w1b": np.ascontiguousarray(W1.reshape(64, 128, NDC, 128).transpose(0, 3, 2, 1)),
        "w3b": np.ascontiguousarray(W3.reshape(64, 128, NDC, 128).transpose(0, 3, 2, 1)),
        "w2p": np.ascontiguousarray(W2.T.reshape(128, 64, D).transpose(1, 0, 2)),
        "bq2": np.ascontiguousarray(bq.reshape(NH, 128).T),
        "bk2": np.ascontiguousarray(bk.reshape(NH, 128).T),
        "bv2": np.ascontiguousarray(bv.reshape(NH, 128).T),
        "bo_r": np.ascontiguousarray(bo.reshape(1, D)),
        "idm": np.eye(128, dtype=np.float32),
        "maskf": np.where(np.arange(128)[:, None] > np.arange(128)[None, :],
                          np.float32(NEGBIG), np.float32(0.0)).astype(np.float32),
        "h128f": _hadamard(128),
        "bd64f": np.kron(np.eye(2, dtype=np.float32), _hadamard(64)).astype(np.float32),
    }
    in_maps = []
    for c in range(8):
        m = dict(shared)
        m["x"] = np.ascontiguousarray(X[c])
        in_maps.append(m)
    return in_maps


def kernel(**inputs):
    in_maps = prep_inputs(inputs)
    nc = build_program()
    res = run_bass_kernel_spmd(nc, in_maps, core_ids=list(range(8)))
    out = np.stack([np.asarray(r["y"], dtype=np.float32) for r in res.results], axis=0)
    return out


# version 14
# speedup vs baseline: 1.0067x; 1.0067x over previous
import numpy as np
import ml_dtypes
import concourse.bass as bass
import concourse.mybir as mybir
from concourse.bass_utils import run_bass_kernel_spmd

F32 = mybir.dt.float32
BF16 = mybir.dt.bfloat16
AF = mybir.ActivationFunctionType
ALU = mybir.AluOpType
AX = mybir.AxisListType

S_TOK = 2048
D = 2048
NH = 16
NDC = 16
NTT = 16
NQG = 4
NPASS = 4
SCALE = 1.0 / (128.0 ** 0.25)
EPS = 1e-5
NRM = float(1.0 / np.sqrt(8192.0))
NEGBIG = -1.0e30
POOL_B = 73728

ENGS = ("sync", "gpsimd", "tensor", "vector", "scalar")
TAPS = {}
PHASE_LIMIT = [None]


class Sch:
    def __init__(self):
        self.ops = {e: [] for e in ENGS}
        self.cnt = {}

    def emit(self, eng, fn, waits=(), inc=None, n=1):
        ws = tuple((s, v) for (s, v) in waits if v is not None and v > 0)
        self.ops[eng].append((fn, ws, inc, n))
        if inc is None:
            return None
        self.cnt[inc] = self.cnt.get(inc, 0) + n
        return self.cnt[inc]

    def c(self, sem):
        return self.cnt.get(sem, 0)


class Guard:
    def __init__(self):
        self.m = {}

    def w(self, *keys):
        return [self.m[k] for k in keys if k in self.m]

    def set(self, key, sem, val):
        self.m[key] = (sem, val)


def _hadamard(n):
    h = np.array([[1.0]], dtype=np.float32)
    while h.shape[0] < n:
        h = np.block([[h, h], [h, -h]])
    return h.astype(np.float32)


def build_program():
    nc = bass.Bass("TRN2", target_bir_lowering=False, debug=False,
                   detect_race_conditions=False)

    x_t = nc.dram_tensor("x", [S_TOK, D], F32, kind="ExternalInput")
    wqb_t = nc.dram_tensor("wqb", [NH, 128, NDC, 128], F32, kind="ExternalInput")
    wkb_t = nc.dram_tensor("wkb", [NH, 128, NDC, 128], F32, kind="ExternalInput")
    wvb_t = nc.dram_tensor("wvb", [4, 128, NDC, 512], F32, kind="ExternalInput")
    wob_t = nc.dram_tensor("wob", [128, NDC, D], F32, kind="ExternalInput")
    w1b_t = nc.dram_tensor("w1b", [64, 128, NDC, 128], F32, kind="ExternalInput")
    w3b_t = nc.dram_tensor("w3b", [64, 128, NDC, 128], F32, kind="ExternalInput")
    w2p_t = nc.dram_tensor("w2p", [64, 128, D], F32, kind="ExternalInput")
    bq2_t = nc.dram_tensor("bq2", [128, NH], F32, kind="ExternalInput")
    bk2_t = nc.dram_tensor("bk2", [128, NH], F32, kind="ExternalInput")
    bv2_t = nc.dram_tensor("bv2", [128, NH], F32, kind="ExternalInput")
    bo_t = nc.dram_tensor("bo_r", [1, D], F32, kind="ExternalInput")
    idm_t = nc.dram_tensor("idm", [128, 128], F32, kind="ExternalInput")
    mask_t = nc.dram_tensor("maskf", [128, 128], F32, kind="ExternalInput")
    h128_t = nc.dram_tensor("h128f", [128, 128], F32, kind="ExternalInput")
    bd64_t = nc.dram_tensor("bd64f", [128, 128], F32, kind="ExternalInput")
    y_t = nc.dram_tensor("y", [S_TOK, D], F32, kind="ExternalOutput")

    def scratch(name, shape, dtype):
        kind = "ExternalOutput" if TAPS.get(name) else "Internal"
        return nc.dram_tensor(name, shape, dtype, kind=kind)

    x1_t = scratch("x1", [S_TOK, D], F32)
    xn1_t = scratch("xn1", [D, S_TOK], BF16)
    w2d_t = scratch("w2d", [8192, D], BF16)

    tap_outs = {}
    for nm, shp in (("xn", [128, NDC, S_TOK]), ("o", [128, NDC, S_TOK]),
                    ("qh", [128, S_TOK]), ("kh", [128, S_TOK]),
                    ("vg", [128, NTT, 512]), ("hid2", [128, NDC, S_TOK])):
        if TAPS.get(nm):
            tap_outs[nm] = nc.dram_tensor("tap_" + nm, shp, BF16, kind="ExternalOutput")

    big0 = nc.alloc_sbuf_tensor("big0", [128, NDC, S_TOK], BF16)
    big1 = nc.alloc_sbuf_tensor("big1", [128, NDC, S_TOK], BF16)
    pool = nc.alloc_sbuf_tensor("pool", [128, POOL_B // 2], BF16)

    idb = nc.alloc_sbuf_tensor("idb", [128, 128], BF16)
    maskb = nc.alloc_sbuf_tensor("maskb", [128, 128], BF16)
    h128b = nc.alloc_sbuf_tensor("h128b", [128, 128], BF16)
    bd64b = nc.alloc_sbuf_tensor("bd64b", [128, 128], BF16)
    onesb = nc.alloc_sbuf_tensor("onesb", [128, 128], BF16)
    bqs = nc.alloc_sbuf_tensor("bqs", [128, NH], F32)
    bks = nc.alloc_sbuf_tensor("bks", [128, NH], F32)
    bvb = nc.alloc_sbuf_tensor("bvb", [128, NH], BF16)
    stats = nc.alloc_sbuf_tensor("stats", [128, 16], F32)
    idf = nc.alloc_sbuf_tensor("idf", [128, 128], F32)

    def pv(off, shape, dtype):
        nel = 1
        for s in shape[1:]:
            nel *= s
        if dtype == BF16:
            ap = pool[0:shape[0], off // 2:off // 2 + nel]
        else:
            ap = pool[0:shape[0], off // 2:off // 2 + nel * 2].bitcast(dtype)
        if len(shape) == 3:
            ap = ap.rearrange("p (a b) -> p a b", a=shape[1])
        elif len(shape) == 4:
            ap = ap.rearrange("p (a b c) -> p a b c", a=shape[1], b=shape[2])
        return ap

    xt = pv(0, [128, 2, D], F32)
    sqt = pv(16384, [128, D], F32)
    vg = pv(0, [128, NTT, 512], BF16)
    qh = pv(16384, [128, S_TOK], BF16)
    kh = pv(20480, [128, S_TOK], BF16)
    pp = pv(24576, [128, 2, 512], BF16)
    mtmp = pv(26624, [128, 64], F32)
    mb1 = pv(69632, [1, 16], BF16)
    zrow = pv(72704, [1, 512], BF16)
    recs = pv(27648, [128, 512], F32)
    t1s = pv(29696, [128, 512], F32)
    w2in = pv(31744, [128, D], BF16)
    w2st = pv(35840, [128, D], BF16)
    wvg = pv(39936, [128, NDC, 512], BF16)
    wqh = pv(56320, [128, NDC, 128], BF16)
    wkh = pv(60416, [128, NDC, 128], BF16)
    pxs = pv(64512, [128, 2, 512], F32)
    xd = pv(24576, [128, D], F32)
    x1t = pv(32768, [128, D], F32)
    xn1st = pv(45056, [128, NDC, 512], BF16)
    bo_row = pv(61440, [1, D], F32)
    bo_eff = pv(69632, [1, D], BF16)
    w1c = pv(0, [128, 2, NDC, 128], BF16)
    w3c = pv(8192, [128, 2, NDC, 128], BF16)
    silu_sb = pv(16384, [128, 2, 512], F32)
    hidt = pv(20480, [128, 2, 512], BF16)
    ytile = pv(22528, [128, 2, D], F32)
    outt = pv(38912, [128, 2, D], F32)

    pb = [nc.alloc_psum_tensor("pb%d" % i, [128, 512], F32) for i in range(8)]

    sch = Sch()
    gd = Guard()
    E = sch.emit
    x = x_t.ap()
    y = y_t.ap()
    x1 = x1_t.ap()

    def dma(eng, sem, dst_ap, src_ap, waits=()):
        tok = E(eng, lambda e, d=dst_ap, s=src_ap: e.dma_start(d, s),
                waits=waits, inc=sem, n=16)
        return (sem, tok)

    dma("gpsimd", "CG", idb[:, :], idm_t.ap())
    dma("gpsimd", "CG", maskb[:, :], mask_t.ap())
    dma("gpsimd", "CG", h128b[:, :], h128_t.ap())
    dma("gpsimd", "CG", bd64b[:, :], bd64_t.ap())
    dma("gpsimd", "CG", bvb[:, :], bv2_t.ap())
    CGF = ("CG", 80)
    E("gpsimd", lambda g: g.memset(zrow[0:1, :], 0.0), inc="G")
    tok_consts = E("gpsimd", lambda g: g.memset(onesb[:, :], 1.0), inc="G")
    dma("sync", "CS", idf[:, :], idm_t.ap())
    dma("sync", "CS", bqs[:, :], bq2_t.ap())
    dma("sync", "CS", bks[:, :], bk2_t.ap())
    CSF = ("CS", 48)
    E("vector", lambda v: v.tensor_scalar(bqs[:, :], bqs[:, :], SCALE, None, ALU.mult),
      waits=[CSF], inc="V")
    E("vector", lambda v: v.tensor_scalar(bks[:, :], bks[:, :], SCALE, None, ALU.mult), inc="V")

    for tt in range(NTT):
        b = tt % 2
        xs = "X%d" % b
        xtb = xt[:, b, :]
        wx = dma("sync", xs, xtb, x[tt * 128:(tt + 1) * 128, :], waits=gd.w(("xt", b)))
        tok_r1 = E("vector", lambda v, xtb=xtb: v.tensor_reduce(stats[:, 0:1], xtb, AX.X, ALU.add),
                   waits=[wx], inc="V")
        tok_m = E("vector", lambda v: v.tensor_scalar(stats[:, 1:2], stats[:, 0:1], 1.0 / D, None, ALU.mult),
                  waits=[("V", tok_r1)], inc="V")
        E("vector", lambda v, xtb=xtb: v.tensor_scalar(xtb, xtb, stats[:, 1:2], None, ALU.subtract),
          waits=[("V", tok_m)])
        E("vector", lambda v, xtb=xtb: v.tensor_mul(sqt[:, :], xtb, xtb),
          waits=gd.w("sqt_xn"))
        tok_r2 = E("vector", lambda v: v.tensor_reduce(stats[:, 2:3], sqt[:, :], AX.X, ALU.add), inc="V")
        tok_var = E("vector", lambda v: v.tensor_scalar(stats[:, 3:4], stats[:, 2:3], 1.0 / D, EPS,
                                                        ALU.mult, ALU.add),
                    waits=[("V", tok_r2)], inc="V")
        tok_std = E("scalar", lambda s: s.activation(stats[:, 4:5], stats[:, 3:4], AF.Sqrt),
                    waits=[("V", tok_var)], inc="S")
        tok_rc = E("vector", lambda v: v.reciprocal(stats[:, 5:6], stats[:, 4:5]),
                   waits=[("S", tok_std)], inc="V")
        tok_xnt = E("vector", lambda v, xtb=xtb: v.tensor_scalar(sqt[:, :], xtb, stats[:, 5:6], None, ALU.mult),
                    waits=[("V", tok_rc)], inc="V")
        gd.set(("xt", b), "V", tok_xnt)
        last_tr = None
        for i in range(NDC):
            kb = 6 + (i % 2)
            tok_tr = E("tensor", lambda t, i=i, kb=kb: t.transpose(
                pb[kb][:, 0:128], sqt[:, i * 128:(i + 1) * 128], idf[:, :]),
                waits=[("V", tok_xnt), CSF] + gd.w(("pb", kb)), inc="T")
            tok_ev = E("scalar", lambda s, i=i, tt=tt, kb=kb: s.copy(
                big0[:, i, tt * 128:(tt + 1) * 128], pb[kb][:, 0:128]),
                waits=[("T", tok_tr)], inc="S")
            gd.set(("pb", kb), "S", tok_ev)
            last_tr = tok_tr
        gd.set("sqt_xn", "T", last_tr)
    tok_xn_ready_S = sch.c("S")

    tap_waits = {}
    if "xn" in tap_outs:
        tap_waits["xn"] = dma("sync", "TP", tap_outs["xn"].ap(), big0[:, :, :],
                              waits=[("S", tok_xn_ready_S)])

    if PHASE_LIMIT[0] == "ln1":
        finals = [(sem, sch.c(sem)) for sem in ("TP",) if sch.c(sem) > 0]
        return _finish(nc, sch, finals)

    prep_state = {"u": 0}

    def emit_prep_chunk():
        u = prep_state["u"]
        if u >= 64:
            return
        prep_state["u"] = u + 1
        win = dma("gpsimd", "PW", w2in[:, :], w2p_t.ap()[u], waits=gd.w("w2in"))
        ev_last = None
        for dg in range(4):
            kb = 6 + (dg % 2)
            tok_mm = E("tensor", lambda t, dg=dg, kb=kb: t.matmul(
                pb[kb][:, :], h128b[:, :], w2in[:, dg * 512:(dg + 1) * 512],
                start=True, stop=True),
                waits=[win, CGF] + gd.w(("pb", kb)) + (gd.w("w2st") if dg == 0 else []),
                inc="T")
            ev_last = E("vector", lambda v, dg=dg, kb=kb: v.tensor_scalar(
                w2st[:, dg * 512:(dg + 1) * 512], pb[kb][:, :], NRM, None, ALU.mult),
                waits=[("T", tok_mm)], inc="V")
            gd.set(("pb", kb), "V", ev_last)
        gd.set("w2in", "T", sch.c("T"))
        out_ap = bass.AP(w2d_t, u * D, [[64 * D, 128], [1, D]])
        wst = dma("sync", "PS", out_ap, w2st[:, :], waits=[("V", ev_last)])
        gd.set("w2st", "PS", wst[1])

    tok_xn_last_T = None
    wv_pref = {}
    for g in range(4):
        if g not in wv_pref:
            wv_pref[g] = dma("gpsimd", "WV", wvg[:, :, :], wvb_t.ap()[g], waits=gd.w("wvg"))
        wwv = wv_pref.pop(g)
        for tt in range(NTT):
            kb = 6 + (tt % 2)
            tok_mm = None
            for dc in range(NDC):
                tok_mm = E("tensor", lambda t, tt=tt, dc=dc, kb=kb: t.matmul(
                    pb[kb][:, :], big0[:, dc, tt * 128:(tt + 1) * 128], wvg[:, dc, :],
                    start=(dc == 0), stop=(dc == NDC - 1)),
                    waits=([("S", tok_xn_ready_S), wwv] + gd.w(("pb", kb)))
                          if dc == 0 else [],
                    inc="T" if dc == NDC - 1 else None)
            tok_ev = E("vector", lambda v, tt=tt, kb=kb: v.tensor_copy(vg[:, tt, :], pb[kb][:, :]),
                       waits=[("T", tok_mm)] + (gd.w("vg") if tt == 0 else [])
                             + ([tap_waits["vg"]] if (g == 1 and tt == 0 and "vg" in tap_waits) else []),
                       inc="V")
            gd.set(("pb", kb), "V", tok_ev)
        tok_vg_ready = sch.c("V")
        gd.set("wvg", "T", sch.c("T"))
        if g + 1 < 4:
            wv_pref[g + 1] = dma("gpsimd", "WV", wvg[:, :, :], wvb_t.ap()[g + 1], waits=gd.w("wvg"))
        if g == 0 and "vg" in tap_outs:
            tap_waits["vg"] = dma("sync", "TP", tap_outs["vg"].ap(), vg[:, :, :],
                                  waits=[("V", tok_vg_ready)])

        for h in range(g * 4, g * 4 + 4):
            if h == 0:
                prefetch = {}
            if h not in prefetch:
                prefetch[h] = (dma("gpsimd", "QW", wqh[:, :, :], wqb_t.ap()[h], waits=gd.w("wqh")),
                               dma("gpsimd", "KW", wkh[:, :, :], wkb_t.ap()[h], waits=gd.w("wkh")))
            wq, wk = prefetch.pop(h)

            def proj_qg(wsb, ww, dst, bias_sb, qg, h=h, base=6):
                kb = base + (qg % 2)
                tok_mm = None
                for dc in range(NDC):
                    tok_mm = E("tensor", lambda t, wsb=wsb, dc=dc, qg=qg, kb=kb: t.matmul(
                        pb[kb][:, :], wsb[:, dc, :], big0[:, dc, qg * 512:(qg + 1) * 512],
                        start=(dc == 0), stop=(dc == NDC - 1)),
                        waits=([ww] + gd.w(("pb", kb))) if dc == 0 else [],
                        inc="T" if dc == NDC - 1 else None)
                tok_ev = E("scalar", lambda s, dst=dst, qg=qg, kb=kb, bias_sb=bias_sb, h=h: s.activation(
                    dst[:, qg * 512:(qg + 1) * 512], pb[kb][:, :], AF.Identity,
                    bias=bias_sb[:, h:h + 1], scale=SCALE),
                    waits=[("T", tok_mm)]
                          + ([tap_waits["qh"]] if (h == 1 and qg == 0 and "qh" in tap_waits) else []),
                    inc="S")
                gd.set(("pb", kb), "S", tok_ev)
                return tok_ev

            E("vector", lambda v: v.memset(mtmp[:, :], NEGBIG))
            for qg in range(NQG):
                proj_qg(wqh, wq, qh, bqs, qg, base=0)
            gd.set("wqh", "T", sch.c("T"))
            qev_tok = sch.c("S")
            tok_flush = None
            mp_flush = {}
            for qg in range(NQG):
                kev = proj_qg(wkh, wk, kh, bks, qg, base=2)
                for ti in range(qg * 4, qg * 4 + 4):
                    kext = (ti + 1) * 128
                    for kcc in range((kext + 511) // 512):
                        w = min(512, kext - kcc * 512)
                        kb = 6 + ((ti + kcc + 1) % 2)
                        tok_mm = E("tensor", lambda t, ti=ti, kcc=kcc, w=w, kb=kb: t.matmul(
                            pb[kb][:, 0:w], qh[:, ti * 128:(ti + 1) * 128],
                            kh[:, kcc * 512:kcc * 512 + w], start=True, stop=True),
                            waits=[("S", max(qev_tok, kev))] + gd.w(("pb", kb)), inc="T")
                        tok_flush = E("vector", lambda v, ti=ti, kcc=kcc, w=w, kb=kb: v.tensor_reduce(
                            mtmp[:, ti * 4 + kcc:ti * 4 + kcc + 1], pb[kb][:, 0:w], AX.X, ALU.max),
                            waits=[("T", tok_mm)], inc="V")
                        gd.set(("pb", kb), "V", tok_flush)
                mp_flush[qg] = tok_flush
            gd.set("wkh", "T", sch.c("T"))
            tok_qk_ready = sch.c("S")
            tok_xn_last_T = sch.c("T")
            if h + 1 < NH:
                prefetch[h + 1] = (dma("gpsimd", "QW", wqh[:, :, :], wqb_t.ap()[h + 1], waits=gd.w("wqh")),
                                   dma("gpsimd", "KW", wkh[:, :, :], wkb_t.ap()[h + 1], waits=gd.w("wkh")))

            if h == 0 and "qh" in tap_outs:
                tap_waits["qh"] = dma("sync", "TP", tap_outs["qh"].ap(), qh[:, :],
                                      waits=[("S", tok_qk_ready)])
                tap_waits["qh"] = ("TP", dma("sync", "TP", tap_outs["kh"].ap(), kh[:, :],
                                             waits=[("S", tok_qk_ready)])[1])

            tok_t1 = None
            for qg in range(NQG):
                tok_t1 = E("vector", lambda v, qg=qg: v.tensor_reduce(
                    stats[:, 12 + qg:13 + qg], mtmp[:, qg * 16:(qg + 1) * 16], AX.X, ALU.max),
                    waits=([("V", mp_flush[3])] + gd.w("bchain")) if qg == 0 else [],
                    inc="V" if qg == NQG - 1 else None)
            tok_t2 = E("tensor", lambda t: t.transpose(
                pb[6][0:4, 0:128], stats[:, 12:16], idf[:, :]),
                waits=[("V", tok_t1)] + gd.w(("pb", 6)), inc="T")
            gd.set("bchain", "T", tok_t2)
            tok_t3 = E("vector", lambda v: v.tensor_reduce(
                stats[0:4, 3:4], pb[6][0:4, 0:128], AX.X, ALU.max),
                waits=[("T", tok_t2)] + gd.w("st34"), inc="V")
            gd.set(("pb", 6), "V", tok_t3)
            tok_t4 = E("tensor", lambda t: t.transpose(
                pb[7][0:1, 0:4], stats[0:4, 3:4], idf[0:4, 0:4]),
                waits=[("V", tok_t3)] + gd.w(("pb", 7)), inc="T")
            gd.set("st34", "T", tok_t4)
            tok_t5 = E("vector", lambda v: v.tensor_scalar(
                mb1[0:1, 0:4], pb[7][0:1, 0:4], -1.0, None, ALU.mult),
                waits=[("T", tok_t4)] + gd.w("mb1"), inc="V")
            gd.set(("pb", 7), "V", tok_t5)
            tok_t6 = E("tensor", lambda t: t.matmul(
                pb[6][:, 0:4], onesb[0:1, :], mb1[0:1, 0:4], start=True, stop=True),
                waits=[("V", tok_t5), ("G", tok_consts)] + gd.w(("pb", 6)), inc="T")
            gd.set("mb1", "T", tok_t6)
            wn = []
            for qg in range(NQG):
                wn += gd.w(("negc", qg))
            tok_t7 = E("vector", lambda v: v.tensor_copy(stats[:, 8:12], pb[6][:, 0:4]),
                       waits=[("T", tok_t6)] + wn, inc="V")
            gd.set(("pb", 6), "V", tok_t7)
            for qg in range(NQG):
                gd.set(("negcv", qg), "V", tok_t7)

            carry = getattr(emit_prep_chunk, "_carry", None)
            if carry is None or carry.get("h") != h:
                carry = {"h": h, 0: {"cp": None, "av": None}, 1: {"cp": None, "av": None}}
            emit_prep_chunk._carry = carry
            for pair in range(2):
                streams = []
                for sidx in range(2):
                    qg = pair * 2 + sidx
                    st_state = {
                        "qg": qg, "s": sidx, "kc": 0, "emitted": 0, "nkc": 4 * (qg + 1),
                        "avb": 4 + sidx, "smb": 6 + sidx,
                        "av_tok": carry[sidx]["av"], "cp_tok": carry[sidx]["cp"],
                        "st_toks": {}, "exp_toks": {}, "cp_toks": {},
                    }
                    for zb in (st_state["avb"], st_state["smb"]):
                        E("tensor", lambda t, zb=zb: t.matmul(
                            pb[zb][:, :], onesb[0:1, :], zrow[0:1, :], start=True, stop=False),
                            waits=[("G", tok_consts)] + gd.w(("pb", zb)), inc=None)
                    streams.append(st_state)

                def emit_st(z, h=h, g=g):
                    qg, sx, kc = z["qg"], z["s"], z["emitted"]
                    stb = 2 * sx + (kc % 2)
                    off = max(0, kc * 128 - qg * 512)
                    wdt = 512 - off
                    is_diag = (kc // 4 == qg)
                    tok_st = E("tensor", lambda t, kc=kc, qg=qg, stb=stb, off=off, wdt=wdt, is_diag=is_diag: t.matmul(
                        pb[stb][:, off:off + wdt], kh[:, kc * 128:(kc + 1) * 128],
                        qh[:, qg * 512 + off:qg * 512 + off + wdt],
                        start=True, stop=not is_diag),
                        waits=gd.w(("pb", stb)) + ([("V", sch.c("V")), CGF] if kc == 0 and qg == 0 else []),
                        inc=None if is_diag else "T")
                    if is_diag:
                        tok_st = E("tensor", lambda t, stb=stb, off=off: t.matmul(
                            pb[stb][:, off:off + 128], idb[:, :], maskb[:, :],
                            start=False, stop=True), inc="T")
                    z["st_toks"][kc] = tok_st
                    wexp = [("T", tok_st)]
                    if kc == 0:
                        wexp += gd.w(("negcv", qg))
                    if z["cp_tok"]:
                        wexp.append(("V", z["cp_tok"]))
                    tok_exp = E("scalar", lambda s, kc=kc, sx=sx, stb=stb, off=off, wdt=wdt, qg=qg: s.activation(
                        pxs[:, sx, 0:wdt], pb[stb][:, off:off + wdt], AF.Exp,
                        bias=stats[:, 8 + qg:9 + qg], scale=1.0),
                        waits=wexp, inc="S")
                    gd.set(("pb", stb), "S", tok_exp)
                    gd.set(("negc", qg), "S", tok_exp)
                    wcp = [("S", tok_exp)]
                    if z["av_tok"]:
                        wcp.append(("T", z["av_tok"]))
                    tok_cp2 = E("vector", lambda v, sx=sx, wdt=wdt: v.tensor_copy(
                        pp[:, sx, 0:wdt], pxs[:, sx, 0:wdt]),
                        waits=wcp, inc="V")
                    z["cp_tok"] = tok_cp2
                    z["cp_toks"][kc] = tok_cp2
                    z["emitted"] += 1

                def emit_acc(z, h=h, g=g):
                    qg, sx, kc, nkc = z["qg"], z["s"], z["kc"], z["nkc"]
                    avb, smb = z["avb"], z["smb"]
                    off = max(0, kc * 128 - qg * 512)
                    wdt = 512 - off
                    tok_cp2 = z["cp_toks"].pop(kc)
                    E("tensor", lambda t, sx=sx, smb=smb, off=off, wdt=wdt, kc=kc, nkc=nkc: t.matmul(
                        pb[smb][:, off:off + wdt], onesb[:, :], pp[:, sx, 0:wdt],
                        start=False, stop=(kc == nkc - 1)),
                        waits=[("V", tok_cp2)], inc=None)
                    tok_av = E("tensor", lambda t, sx=sx, avb=avb, off=off, wdt=wdt, kc=kc, nkc=nkc, h=h, g=g: t.matmul(
                        pb[avb][:, off:off + wdt], vg[:, kc, (h - g * 4) * 128:(h - g * 4 + 1) * 128],
                        pp[:, sx, 0:wdt],
                        start=False, stop=(kc == nkc - 1)),
                        waits=([("V", tok_vg_ready)] if kc == 0 else []), inc="T")
                    z["av_tok"] = tok_av
                    z["kc"] += 1

                def emit_tail(z, h=h):
                    qg, avb, smb = z["qg"], z["avb"], z["smb"]
                    tok_avstop = z["av_tok"]
                    tok_cp = E("vector", lambda v, smb=smb: v.tensor_scalar(
                        t1s[:, :], pb[smb][:, :], 1e-38, None, ALU.max),
                        waits=[("T", tok_avstop)] + gd.w("t1s"), inc="V")
                    gd.set(("pb", smb), "V", tok_cp)
                    tok_rc = E("vector", lambda v: v.reciprocal(recs[:, :], t1s[:, :]),
                               waits=[("V", tok_cp)] + gd.w("recs"), inc="V")
                    tok_oev = E("vector", lambda v, h=h, qg=qg, avb=avb: v.tensor_tensor(
                        big1[:, h, qg * 512:(qg + 1) * 512], pb[avb][:, :], recs[:, :], ALU.mult),
                        waits=[("V", tok_rc), ("T", tok_avstop)], inc="V")
                    gd.set(("pb", avb), "V", tok_oev)
                    gd.set("t1s", "V", tok_rc)
                    gd.set("recs", "V", tok_oev)

                za, zb_ = streams
                for z in streams:
                    if z["emitted"] < z["nkc"]:
                        emit_st(z)
                while za["kc"] < za["nkc"] or zb_["kc"] < zb_["nkc"]:
                    for z in (za, zb_):
                        if z["emitted"] < z["nkc"]:
                            emit_st(z)
                        if z["kc"] < z["emitted"]:
                            emit_acc(z)
                emit_tail(za)
                emit_tail(zb_)
                for z in streams:
                    carry[z["s"]]["cp"] = z["cp_tok"]
                    carry[z["s"]]["av"] = z["av_tok"]
            for _ in range(4):
                emit_prep_chunk()
    tok_o_done_V = sch.c("V")
    tok_attn_T = sch.c("T")
    PSF = ("PS", sch.c("PS"))

    if "o" in tap_outs:
        tap_waits["o"] = dma("sync", "TP", tap_outs["o"].ap(), big1[:, :, :],
                             waits=[("V", tok_o_done_V)])
    if PHASE_LIMIT[0] == "attn":
        finals = [(sem, sch.c(sem)) for sem in ("TP", "PS") if sch.c(sem) > 0]
        return _finish(nc, sch, finals)

    wo_waits = []
    for c4 in range(4):
        in_ap = bass.AP(wob_t, c4 * 4 * D, [[NDC * D, 128], [D, 4], [1, D]])
        wo_waits.append(dma("gpsimd", "WO%d" % c4, big0[:, c4 * 4:(c4 + 1) * 4, :], in_ap,
                            waits=[("T", tok_xn_last_T)]))

    wbo = dma("sync", "BO", bo_row, bo_t.ap(),
              waits=[("T", tok_attn_T), PSF, ("V", tok_o_done_V)])
    tok_wobv = None
    for og in range(4):
        for ic in range(NDC):
            tok_wobv = E("tensor", lambda t, og=og, ic=ic: t.matmul(
                pb[6][0:1, :], bvb[:, ic:ic + 1], big0[:, ic, og * 512:(og + 1) * 512],
                start=(ic == 0), stop=(ic == NDC - 1)),
                waits=([wo_waits[ic // 4], CGF] + (gd.w(("pb", 6)) if ic == 0 else []))
                      if ic % 4 == 0 else [],
                inc="T" if ic == NDC - 1 else None)
        tok_be = E("vector", lambda v, og=og: v.tensor_tensor(
            bo_eff[0:1, og * 512:(og + 1) * 512], bo_row[0:1, og * 512:(og + 1) * 512],
            pb[6][0:1, :], ALU.add),
            waits=[("T", tok_wobv), wbo], inc="V")
        gd.set(("pb", 6), "V", tok_be)
    tok_boeff = sch.c("V")

    xn1_flush_tok = None
    a4_pend = []
    _a4_state = {"fl": None}

    def _flush_a4tr(item):
        tt, bs, tok_xnt = item
        last_tr = None
        for i in range(NDC):
            kb = bs + (i % 2)
            tok_tr = E("tensor", lambda t, i=i, kb=kb: t.transpose(
                pb[kb][:, 0:128], x1t[:, i * 128:(i + 1) * 128], idf[:, :]),
                waits=[("V", tok_xnt)] + gd.w(("pb", kb)), inc="T")
            tok_ev2 = E("scalar", lambda s, i=i, tt=tt, kb=kb: s.copy(
                xn1st[:, i, (tt % 4) * 128:(tt % 4 + 1) * 128], pb[kb][:, 0:128]),
                waits=[("T", tok_tr)] + (gd.w("xn1st") if i == 0 else []), inc="S")
            gd.set(("pb", kb), "S", tok_ev2)
            last_tr = tok_tr
        gd.set(("x1t", "tr"), "T", last_tr)
        if tt % 4 == 3:
            tb = tt // 4
            out_ap = bass.AP(xn1_t, tb * 512, [[S_TOK, 128], [128 * S_TOK, NDC], [1, 512]])
            wfl = dma("sync", "SN", out_ap, xn1st[:, :, :], waits=[("S", sch.c("S"))])
            gd.set("xn1st", "SN", wfl[1])
            _a4_state["fl"] = wfl[1]

    for tt in range(NTT):
        b = tt % 2
        xs = "X%d" % b
        xtb = xt[:, b, :]
        wx = dma("sync", xs, xtb, x[tt * 128:(tt + 1) * 128, :],
                 waits=gd.w(("xt", b)) + ([("T", tok_attn_T)] if tt < 2 else []))
        bs = 0 if b == 0 else 4
        for og in range(4):
            E("tensor", lambda t, og=og, bs=bs: t.matmul(
                pb[bs + og][:, :], onesb[0:1, :], bo_eff[0:1, og * 512:(og + 1) * 512],
                start=True, stop=False),
                waits=[("V", tok_boeff)] + gd.w(("pb", bs + og)), inc=None)
        tok_mm = None
        for ic in range(NDC):
            for og in range(4):
                tok_mm = E("tensor", lambda t, tt=tt, ic=ic, og=og, bs=bs: t.matmul(
                    pb[bs + og][:, :], big1[:, ic, tt * 128:(tt + 1) * 128],
                    big0[:, ic, og * 512:(og + 1) * 512],
                    start=False, stop=(ic == NDC - 1)),
                    waits=[("V", tok_o_done_V), wo_waits[ic // 4]] if og == 0 and ic % 4 == 0 else [],
                    inc="T" if (ic == NDC - 1 and og == 3) else None)
        if len(a4_pend) > 0:
            _flush_a4tr(a4_pend.pop(0))
        tok_evs = []
        for og in range(4):
            tok_ev = E("vector", lambda v, og=og, bs=bs, xtb=xtb: v.tensor_tensor(
                x1t[:, og * 512:(og + 1) * 512], pb[bs + og][:, :], xtb[:, og * 512:(og + 1) * 512],
                ALU.add),
                waits=[("T", tok_mm), wx] + ((gd.w(("x1t", "st"), ("x1t", "tr")) + [PSF]) if og == 0 else []),
                inc="V")
            gd.set(("pb", bs + og), "V", tok_ev)
            tok_evs.append(tok_ev)
        gd.set(("xt", b), "V", tok_evs[-1])
        wx1 = dma("sync", "SX", x1[tt * 128:(tt + 1) * 128, :], x1t[:, :],
                  waits=[("V", tok_evs[-1])])
        gd.set(("x1t", "st"), "SX", wx1[1])
        tok_r1 = E("vector", lambda v: v.tensor_reduce(stats[:, 0:1], x1t[:, :], AX.X, ALU.add), inc="V")
        tok_m = E("vector", lambda v: v.tensor_scalar(stats[:, 1:2], stats[:, 0:1], 1.0 / D, None, ALU.mult),
                  waits=[("V", tok_r1)], inc="V")
        E("vector", lambda v: v.tensor_scalar(xd[:, :], x1t[:, :], stats[:, 1:2], None, ALU.subtract),
          waits=[("V", tok_m)])
        E("vector", lambda v: v.tensor_mul(sqt[:, :], xd[:, :], xd[:, :]),
          waits=gd.w("sqt_xn"))
        tok_r2 = E("vector", lambda v: v.tensor_reduce(stats[:, 2:3], sqt[:, :], AX.X, ALU.add), inc="V")
        tok_var = E("vector", lambda v: v.tensor_scalar(stats[:, 3:4], stats[:, 2:3], 1.0 / D, EPS,
                                                        ALU.mult, ALU.add),
                    waits=[("V", tok_r2)], inc="V")
        tok_std = E("scalar", lambda s: s.activation(stats[:, 4:5], stats[:, 3:4], AF.Sqrt),
                    waits=[("V", tok_var)], inc="S")
        tok_rc = E("vector", lambda v: v.reciprocal(stats[:, 5:6], stats[:, 4:5]),
                   waits=[("S", tok_std)], inc="V")
        tok_xnt = E("vector", lambda v: v.tensor_scalar(x1t[:, :], xd[:, :], stats[:, 5:6], None, ALU.mult),
                    waits=[("V", tok_rc), wx1], inc="V")
        a4_pend.append((tt, bs, tok_xnt))
    while a4_pend:
        _flush_a4tr(a4_pend.pop(0))

    if PHASE_LIMIT[0] == "a4":
        finals = [(sem, sch.c(sem)) for sem in ("TP", "PS", "SX", "SN") if sch.c(sem) > 0]
        return _finish(nc, sch, finals)

    SXF = ("SX", sch.c("SX"))
    SNF = ("SN", sch.c("SN"))
    tok_big1_free_T = sch.c("T")
    tok_big0_free_T = sch.c("T")
    ffn_fence = [("T", sch.c("T")), ("V", sch.c("V")), ("S", sch.c("S")), SXF, SNF, PSF]

    def _flush_bd(item):
        hc, tg, sb2, bb, tok_hid, first = item
        tok_bd = E("tensor", lambda t, sb2=sb2, bb=bb: t.matmul(
            pb[bb][:, :], bd64b[:, :], hidt[:, sb2, :], start=True, stop=True),
            waits=[("V", tok_hid)] + gd.w(("pb", bb)), inc="T")
        gd.set(("hidt", sb2), "T", tok_bd)
        tok_h2 = E("vector", lambda v, hc=hc, tg=tg, bb=bb: v.tensor_copy(
            big0[:, hc, tg * 512:(tg + 1) * 512], pb[bb][:, :]),
            waits=[("T", tok_bd)] +
                  ([("T", tok_big0_free_T)] if first else []),
            inc="V")
        gd.set(("pb", bb), "V", tok_h2)

    for p in range(NPASS):
        bd_pend = []
        in_ap = bass.AP(xn1_t, 0, [[S_TOK, 128], [128 * S_TOK, NDC], [1, S_TOK]])
        wxn1 = dma("sync", "XN", big1[:, :, :], in_ap,
                   waits=[SNF, ("T", tok_big1_free_T)]
                         + ([tap_waits["o"]] if (p == 0 and "o" in tap_waits) else []))
        assert _a4_state["fl"] is not None
        for hc in range(NDC):
            ch = p * NDC + hc
            hb = hc % 2
            fence = ffn_fence if (p == 0 and hc < 2) else []
            ww1 = dma("gpsimd", "A%d" % hb, w1c[:, hb, :, :], w1b_t.ap()[ch],
                      waits=list(fence) + gd.w(("w1c", hb)))
            ww3 = dma("gpsimd", "B%d" % hb, w3c[:, hb, :, :], w3b_t.ap()[ch],
                      waits=list(fence) + gd.w(("w3c", hb)))
            for tg in range(NQG):
                rot = (hc * 4 + tg) % 2
                gb, lb, bb = rot, 2 + rot, 4 + rot
                tok_g = tok_l = None
                for dc in range(NDC):
                    tok_g = E("tensor", lambda t, hb=hb, dc=dc, tg=tg, gb=gb: t.matmul(
                        pb[gb][:, :], w1c[:, hb, dc, :], big1[:, dc, tg * 512:(tg + 1) * 512],
                        start=(dc == 0), stop=(dc == NDC - 1)),
                        waits=([ww1, wxn1] + gd.w(("pb", gb))) if dc == 0 else [],
                        inc="T" if dc == NDC - 1 else None)
                for dc in range(NDC):
                    tok_l = E("tensor", lambda t, hb=hb, dc=dc, tg=tg, lb=lb: t.matmul(
                        pb[lb][:, :], w3c[:, hb, dc, :], big1[:, dc, tg * 512:(tg + 1) * 512],
                        start=(dc == 0), stop=(dc == NDC - 1)),
                        waits=([ww3] + gd.w(("pb", lb))) if dc == 0 else [],
                        inc="T" if dc == NDC - 1 else None)
                sb2 = rot
                tok_silu = E("scalar", lambda s, sb2=sb2, gb=gb: s.activation(
                    silu_sb[:, sb2, :], pb[gb][:, :], AF.Silu),
                    waits=[("T", tok_g)] + gd.w(("silu", sb2)), inc="S")
                gd.set(("pb", gb), "S", tok_silu)
                tok_hid = E("vector", lambda v, sb2=sb2, lb=lb: v.tensor_tensor(
                    hidt[:, sb2, :], silu_sb[:, sb2, :], pb[lb][:, :], ALU.mult),
                    waits=[("S", tok_silu), ("T", tok_l)] + gd.w(("hidt", sb2)), inc="V")
                gd.set(("pb", lb), "V", tok_hid)
                gd.set(("silu", sb2), "V", tok_hid)
                bd_pend.append((hc, tg, sb2, bb, tok_hid,
                                (p == 0 and hc == 0 and tg == 0)))
                if len(bd_pend) > 1:
                    _flush_bd(bd_pend.pop(0))
            gd.set(("w1c", hb), "T", sch.c("T"))
            gd.set(("w3c", hb), "T", sch.c("T"))
        while bd_pend:
            _flush_bd(bd_pend.pop(0))
        tok_stage1_T = sch.c("T")
        tok_stage1_V = sch.c("V")
        if p == 0 and "hid2" in tap_outs:
            dma("sync", "TP", tap_outs["hid2"].ap(), big0[:, :, :], waits=[("V", tok_stage1_V)])

        w2dl = []
        for q4 in range(4):
            in2 = bass.AP(w2d_t, (p * NDC + q4 * 4) * 128 * D,
                          [[D, 128], [128 * D, 4], [1, D]])
            w2dl.append(dma("sync", "D%d" % q4, big1[:, q4 * 4:(q4 + 1) * 4, :], in2,
                            waits=[("T", tok_stage1_T), PSF]))
        sy_full = ("SY", sch.c("SY"))
        for tc in range(NTT):
            b = tc % 2
            bs = 0 if b == 0 else 4
            ytb = ytile[:, b, :]
            if p == 0:
                wy = dma("sync", "Y%d" % b, ytb, x1[tc * 128:(tc + 1) * 128, :],
                         waits=[SXF] + (ffn_fence if tc < 2 else []) + gd.w(("ytile", b)))
            else:
                wy = dma("sync", "Y%d" % b, ytb, y[tc * 128:(tc + 1) * 128, :],
                         waits=[sy_full] + gd.w(("ytile", b)))
            tok_mm = None
            for hc in range(NDC):
                for dg in range(4):
                    tok_mm = E("tensor", lambda t, tc=tc, hc=hc, dg=dg, bs=bs: t.matmul(
                        pb[bs + dg][:, :], big0[:, hc, tc * 128:(tc + 1) * 128],
                        big1[:, hc, dg * 512:(dg + 1) * 512],
                        start=(hc == 0), stop=(hc == NDC - 1)),
                        waits=([w2dl[hc // 4], ("V", tok_stage1_V)] + gd.w(("pb", bs + dg)))
                              if dg == 0 else [],
                        inc="T" if (hc == NDC - 1 and dg == 3) else None)
            tok_evs = []
            for dg in range(4):
                tok_ev = E("vector", lambda v, dg=dg, bs=bs, ytb=ytb, b=b: v.tensor_tensor(
                    outt[:, b, dg * 512:(dg + 1) * 512], pb[bs + dg][:, :],
                    ytb[:, dg * 512:(dg + 1) * 512], ALU.add),
                    waits=[("T", tok_mm), wy] + (gd.w(("outt", b)) if dg == 0 else []),
                    inc="V")
                gd.set(("pb", bs + dg), "V", tok_ev)
                tok_evs.append(tok_ev)
            gd.set(("ytile", b), "V", tok_evs[-1])
            wsy = dma("sync", "SY", y[tc * 128:(tc + 1) * 128, :], outt[:, b, :],
                      waits=[("V", tok_evs[-1])])
            gd.set(("outt", b), "SY", wsy[1])
        tok_big1_free_T = sch.c("T")
        tok_big0_free_T = sch.c("T")

    finals = [(s, sch.c(s)) for s in ("SY", "SX", "SN", "PS", "TP") if sch.c(s) > 0]
    return _finish(nc, sch, finals)


def _finish(nc, sch, finals):
    names = set(sch.cnt)
    for ops in sch.ops.values():
        for _, ws, _, _ in ops:
            for s, v in ws:
                names.add(s)
    import contextlib
    with contextlib.ExitStack() as stack:
        sems = {nm: stack.enter_context(nc.semaphore(nm)) for nm in sorted(names)}
        block = stack.enter_context(nc.Block())

        def replay(ename, final_waits=()):
            ops = sch.ops[ename]

            def run(eng):
                seen = {}
                for fn, waits, inc, n in ops:
                    for s, v in waits:
                        if seen.get(s, 0) < v:
                            eng.wait_ge(sems[s], v)
                            seen[s] = v
                    ins = fn(eng)
                    if inc is not None:
                        ins.then_inc(sems[inc], n)
                for s, v in final_waits:
                    if seen.get(s, 0) < v:
                        eng.wait_ge(sems[s], v)
            return run

        block.sync(replay("sync", final_waits=finals))
        block.gpsimd(replay("gpsimd"))
        block.tensor(replay("tensor"))
        block.vector(replay("vector"))
        block.scalar(replay("scalar"))

    return nc


def prep_inputs(inputs):
    X = np.ascontiguousarray(np.asarray(inputs["L0_attn_X"], dtype=np.float32))
    Wq = np.asarray(inputs["L0_attn_Wq"], dtype=np.float32)
    Wk = np.asarray(inputs["L0_attn_Wk"], dtype=np.float32)
    Wv = np.asarray(inputs["L0_attn_Wv"], dtype=np.float32)
    Wo = np.asarray(inputs["L0_attn_Wo"], dtype=np.float32)
    W1 = np.asarray(inputs["L0_ffn_W1"], dtype=np.float32)
    W2 = np.asarray(inputs["L0_ffn_W2"], dtype=np.float32)
    W3 = np.asarray(inputs["L0_ffn_W3"], dtype=np.float32)
    bq = np.asarray(inputs["L0_attn_bq"], dtype=np.float32)
    bk = np.asarray(inputs["L0_attn_bk"], dtype=np.float32)
    bv = np.asarray(inputs["L0_attn_bv"], dtype=np.float32)
    bo = np.asarray(inputs["L0_attn_bo"], dtype=np.float32)

    shared = {
        "wqb": np.ascontiguousarray(Wq.reshape(NH, 128, NDC, 128).transpose(0, 3, 2, 1)),
        "wkb": np.ascontiguousarray(Wk.reshape(NH, 128, NDC, 128).transpose(0, 3, 2, 1)),
        "wvb": np.ascontiguousarray(Wv.reshape(4, 512, NDC, 128).transpose(0, 3, 2, 1)),
        "wob": np.ascontiguousarray(Wo.reshape(D, NDC, 128).transpose(2, 1, 0)),
        "w1b": np.ascontiguousarray(W1.reshape(64, 128, NDC, 128).transpose(0, 3, 2, 1)),
        "w3b": np.ascontiguousarray(W3.reshape(64, 128, NDC, 128).transpose(0, 3, 2, 1)),
        "w2p": np.ascontiguousarray(W2.T.reshape(128, 64, D).transpose(1, 0, 2)),
        "bq2": np.ascontiguousarray(bq.reshape(NH, 128).T),
        "bk2": np.ascontiguousarray(bk.reshape(NH, 128).T),
        "bv2": np.ascontiguousarray(bv.reshape(NH, 128).T),
        "bo_r": np.ascontiguousarray(bo.reshape(1, D)),
        "idm": np.eye(128, dtype=np.float32),
        "maskf": np.where(np.arange(128)[:, None] > np.arange(128)[None, :],
                          np.float32(NEGBIG), np.float32(0.0)).astype(np.float32),
        "h128f": _hadamard(128),
        "bd64f": np.kron(np.eye(2, dtype=np.float32), _hadamard(64)).astype(np.float32),
    }
    in_maps = []
    for c in range(8):
        m = dict(shared)
        m["x"] = np.ascontiguousarray(X[c])
        in_maps.append(m)
    return in_maps


def kernel(**inputs):
    in_maps = prep_inputs(inputs)
    nc = build_program()
    res = run_bass_kernel_spmd(nc, in_maps, core_ids=list(range(8)))
    out = np.stack([np.asarray(r["y"], dtype=np.float32) for r in res.results], axis=0)
    return out
